# version 03 (final)
# baseline (speedup 1.0000x reference)
import contextlib
import numpy as np
import concourse.bass as bass
import concourse.mybir as mybir
from concourse.alu_op_type import AluOpType as OP
from concourse.bass_utils import run_bass_kernel_spmd
from concourse.bass_core_rust import IndirectOffsetOnAxis

F32 = mybir.dt.float32
ACT = mybir.ActivationFunctionType
AX = mybir.AxisListType.X

D = 1024
T = 4096
G = 8
CT = 512
NCH = T // CT
I2 = 2048
NST = 32
HEADS = 8
DH = 128
E = 8
CAPE = 1536
NCHS = (E * CAPE) // CT
TOTSLOT = E * CAPE + 128
LN_EPS = 1e-5

_UPTO = "full"
_P2STOP = None
_P3STOP = None
_P4STOP = None
_NCORES = 8

_ENGS = ("scalar", "vector", "tensor", "gpsimd")


class Build:

    def __init__(self, nc):
        self.nc = nc
        self.st = contextlib.ExitStack()
        self.sched = []
        self.sems = {}
        for name in ("w", "g", "pe", "ve", "sc", "s2"):
            self.sems[name] = [self.st.enter_context(nc.semaphore("sem_" + name)), 0]

    def tick(self, name, n=1):
        s = self.sems[name]
        s[1] += n
        return (s[0], n)

    def val(self, name):
        return self.sems[name][1]

    def wait(self, eng, name, v=None):
        s = self.sems[name]
        vv = s[1] if v is None else v
        if vv <= 0:
            return
        self.sched.append((eng, lambda e, h=s[0], t=vv: e.wait_ge(h, t)))

    def emit(self, eng, fn, inc=None):
        if inc is None:
            self.sched.append((eng, fn))
        else:
            def f(e, fn=fn, inc=inc):
                i = fn(e)
                i.then_inc(inc[0], inc[1])
            self.sched.append((eng, f))

    def flush(self):
        nc, sched = self.nc, self.sched
        self.sched = []
        with nc.Block() as block:
            for eng_name in _ENGS:
                def mk(en):
                    def body(eng):
                        for e, fn in sched:
                            if e == en:
                                fn(eng)
                    return body
                getattr(block, eng_name)(mk(eng_name))

    def sb(self, name, shape, dtype=F32):
        st = getattr(self, "cur_st", None) or self.st
        return st.enter_context(self.nc.sbuf_tensor(name, shape, dtype))

    def ps(self, name, shape, dtype=F32):
        return self.st.enter_context(self.nc.psum_tensor(name, shape, dtype))

    @contextlib.contextmanager
    def arena(self):
        st = contextlib.ExitStack()
        prev = getattr(self, "cur_st", None)
        self.cur_st = st
        try:
            yield
        finally:
            self.cur_st = prev
            st.close()

    def dma(self, eng, out_ap, in_ap, sem="g"):
        inc = self.tick(sem, 16)
        self.emit(eng, lambda e, o=out_ap, i=in_ap: e.dma_start(o, i), inc=inc)
        return self.val(sem)

    def vop(self, fn, sem=None):
        inc = self.tick(sem) if sem else None
        self.emit("vector", fn, inc=inc)
        return self.val(sem) if sem else None

    def sop(self, fn, sem=None):
        inc = self.tick(sem) if sem else None
        self.emit("scalar", fn, inc=inc)
        return self.val(sem) if sem else None

    def pop(self, fn, sem=None):
        inc = self.tick(sem) if sem else None
        self.emit("tensor", fn, inc=inc)
        return self.val(sem) if sem else None


def build_nc(upto="full"):
    nc = bass.Bass("TRN2", target_bir_lowering=False)
    B = Build(nc)
    dram = {}

    def din(name, shape):
        dram[name] = nc.dram_tensor(name, list(shape), F32, kind="ExternalInput").ap()
        return dram[name]

    def dscr(name, shape):
        dram[name] = nc.dram_tensor(name, list(shape), F32, kind="Internal").ap()
        return dram[name]

    def dout(name, shape):
        dram[name] = nc.dram_tensor(name, list(shape), F32, kind="ExternalOutput").ap()
        return dram[name]

    xT = din("xT", (D, T))
    w_in = din("w_in", (D, I2))
    b_in = din("b_in", (128, I2 // 128))
    conv_w = din("conv_w", (128, G * 4))
    conv_b = din("conv_b", (128, G))
    A_log = din("A_log", (128, G * NST))
    w_B = din("w_B", (D, NST))
    b_B = din("b_B", (NST, 1))
    w_C = din("w_C", (D, NST))
    b_C = din("b_C", (NST, 1))
    w_dt = din("w_dt", (D, D))
    b_dt = din("b_dt", (128, G))
    w_out = din("w_out", (D, D))
    b_out = din("b_out", (128, G))
    ln_g = din("ln_g", (128, G))
    ln_b = din("ln_b", (128, G))
    w_q = din("w_q", (D, D))
    w_k = din("w_k", (D, D))
    w_v = din("w_v", (D, D))
    w_o = din("w_o", (D, D))
    gate_w = din("gate_w", (D, E))
    w1 = din("w1", (E, D, D))
    b1 = din("b1", (E, 128, G))
    w2 = din("w2", (E, D, D))
    b2 = din("b2", (E, 128, G))
    w3 = din("w3", (E, D, D))
    b3 = din("b3", (E, 128, G))
    sw1 = din("sw1", (D, I2))
    sw2 = din("sw2", (D, I2))
    sw3 = din("sw3", (I2, D))
    eye = din("eye", (128, 128))
    caus = din("caus", (128, 128))
    cos_t = din("cos_t", (64, T))
    sin_t = din("sin_t", (64, T))
    cosq_t = din("cosq_t", (64, T))
    sinq_t = din("sinq_t", (64, T))
    tokid = nc.dram_tensor("tokid", [128, 32], mybir.dt.int32,
                           kind="ExternalInput").ap()
    dram["tokid"] = tokid
    ebase = din("ebase", (40, 1))

    xz_s = dscr("xz_s", (I2, T))
    xtok_s = dscr("xtok_s", (T + 128, D))
    ytok_s = dscr("ytok_s", (T + 128, D))
    tok_s = nc.dram_tensor("tok_s", [TOTSLOT, 1], mybir.dt.int32, kind="Internal").ap()
    wslot_s = dscr("wslot_s", (TOTSLOT, 1))
    resid_s = dscr("resid_s", (D, T))
    xln_s = dscr("xln_s", (D, T))
    g_s = dscr("g_s", (I2, T))
    y_s = dscr("y_s", (D, T))
    q_s = dscr("q_s", (D, T))
    k_s = dscr("k_s", (D, T))
    v_s = dscr("v_s", (HEADS, T, DH))
    o_s = dscr("o_s", (D, T))

    outputs = {}
    if upto == "full":
        outputs["outT"] = dout("outT", (D, T))
    else:
        dump_shapes = {"p1": (I2, T), "p2": (D, T), "p3": (D, T), "p4": (D, T)}
        outputs["dbg"] = dout("dbg", dump_shapes[upto])

    sb_eye = B.sb("sb_eye", [128, 128])
    sb_caus = B.sb("sb_caus", [128, 128])
    sb_ones_col = B.sb("sb_ones_col", [128, 1])
    sb_ones_row = B.sb("sb_ones_row", [1, 128])
    sb_binc = B.sb("sb_binc", [128, 16])
    sb_convw = B.sb("sb_convw", [128, G, 4])
    sb_convb = B.sb("sb_convb", [128, G])
    sb_A = B.sb("sb_A", [128, G, NST])
    sb_bB = B.sb("sb_bB", [NST, 1])
    sb_bC = B.sb("sb_bC", [NST, 1])
    sb_bdt = B.sb("sb_bdt", [128, G])
    sb_bout = B.sb("sb_bout", [128, G])
    sb_lng = B.sb("sb_lng", [128, G])
    sb_lnb = B.sb("sb_lnb", [128, G])
    sb_gatew = B.sb("sb_gatew", [128, G, E])
    sb_wB = B.sb("sb_wB", [128, G, NST])
    sb_wC = B.sb("sb_wC", [128, G, NST])
    row_tmp = B.sb("row_tmp", [1, 6, CT])
    sq = B.sb("sq", [128, CT])
    bc_rs = B.sb("bc_rs", [128, CT])
    bc_mrs = B.sb("bc_mrs", [128, CT])
    evac = B.sb("evac", [128, 2, CT])

    ps_mm = B.ps("ps_mm", [128, 2, CT])
    ps_mm2 = B.ps("ps_mm2", [128, 2, CT])
    ps_aux = B.ps("ps_aux", [128, 2, CT])
    ps_bc = B.ps("ps_bc", [128, 2, CT])

    B.dma("scalar", sb_eye[:, :], eye, sem="w")
    B.dma("scalar", sb_caus[:, :], caus, sem="w")
    B.dma("scalar", sb_binc[:, :], b_in, sem="w")
    B.dma("scalar", sb_convw.ap().rearrange("p g k -> p (g k)"), conv_w, sem="w")
    B.dma("scalar", sb_convb[:, :], conv_b, sem="w")
    B.dma("scalar", sb_A.ap().rearrange("p g n -> p (g n)"), A_log, sem="w")
    B.dma("scalar", sb_bB[:, :], b_B, sem="w")
    B.dma("scalar", sb_bC[:, :], b_C, sem="w")
    B.dma("scalar", sb_bdt[:, :], b_dt, sem="w")
    B.dma("scalar", sb_bout[:, :], b_out, sem="w")
    B.dma("scalar", sb_lng[:, :], ln_g, sem="w")
    B.dma("scalar", sb_lnb[:, :], ln_b, sem="w")
    B.dma("scalar", sb_wB.ap(), w_B.rearrange("(g p) n -> p g n", p=128), sem="w")
    B.dma("scalar", sb_wC.ap(), w_C.rearrange("(g p) n -> p g n", p=128), sem="w")
    wv = B.dma("scalar", sb_gatew.ap(),
               gate_w.rearrange("(g p) e -> p g e", p=128), sem="w")
    B.wait("vector", "w", wv)
    B.vop(lambda e: nc.vector.memset(sb_ones_col[:, :], 1.0))
    B.vop(lambda e: nc.vector.memset(sb_ones_row[:, :], 1.0))
    B.vop(lambda e: nc.vector.tensor_scalar(sb_A[:], sb_A[:], 2.0, -5.0, OP.min, OP.max))
    sv = B.sop(lambda e: nc.scalar.activation(sb_A[:], sb_A[:], ACT.Exp), sem="sc")
    B.wait("vector", "sc", sv)
    B.vop(lambda e: nc.vector.tensor_scalar(sb_A[:], sb_A[:], -1.0, None, OP.mult, OP.bypass),
          sem="ve")
    B.flush()

    state = {"aux_free": 0, "sq_free": 0}

    def ln_stats(c, src):
        B.wait("tensor", "ve", state["aux_free"])
        B.wait("scalar", "ve")
        B.wait("tensor", "ve")
        for g in range(G):
            B.wait("scalar", "pe", state["sq_free"])
            svv = B.sop(lambda e, g=g: nc.scalar.activation(sq[:, :], src[:, g, :], ACT.Square),
                        sem="sc")
            B.wait("tensor", "sc", svv)
            B.pop(lambda e, g=g: nc.tensor.matmul(
                ps_aux[0:1, 0, :], sb_ones_col[:, :], src[:, g, :],
                start=(g == 0), stop=(g == G - 1)))
            pv = B.pop(lambda e, g=g: nc.tensor.matmul(
                ps_aux[0:1, 1, :], sb_ones_col[:, :], sq[:, :],
                start=(g == 0), stop=(g == G - 1)), sem="pe")
            state["sq_free"] = pv
        B.wait("vector", "pe", pv)
        B.vop(lambda e: nc.vector.tensor_scalar(row_tmp[0:1, 0, :], ps_aux[0:1, 0, :],
                                                1.0 / D, None, OP.mult, OP.bypass))
        B.vop(lambda e: nc.vector.tensor_scalar(row_tmp[0:1, 1, :], ps_aux[0:1, 1, :],
                                                1.0 / D, None, OP.mult, OP.bypass))
        B.vop(lambda e: nc.vector.tensor_tensor(row_tmp[0:1, 2, :], row_tmp[0:1, 0, :],
                                                row_tmp[0:1, 0, :], OP.mult))
        B.vop(lambda e: nc.vector.tensor_tensor(row_tmp[0:1, 2, :], row_tmp[0:1, 1, :],
                                                row_tmp[0:1, 2, :], OP.subtract))
        vv = B.vop(lambda e: nc.vector.tensor_scalar(row_tmp[0:1, 2, :], row_tmp[0:1, 2, :],
                                                     LN_EPS, None, OP.add, OP.bypass), sem="ve")
        B.wait("scalar", "ve", vv)
        svv = B.sop(lambda e: nc.scalar.activation(row_tmp[0:1, 3, :], row_tmp[0:1, 2, :],
                                                   ACT.Sqrt), sem="sc")
        B.wait("vector", "sc", svv)
        B.vop(lambda e: nc.vector.reciprocal(row_tmp[0:1, 3, :], row_tmp[0:1, 3, :]))
        B.vop(lambda e: nc.vector.tensor_tensor(row_tmp[0:1, 4, :], row_tmp[0:1, 3, :],
                                                row_tmp[0:1, 3, :], OP.mult))
        B.vop(lambda e: nc.vector.tensor_tensor(row_tmp[0:1, 4, :], row_tmp[0:1, 2, :],
                                                row_tmp[0:1, 4, :], OP.mult))
        B.vop(lambda e: nc.vector.tensor_scalar(row_tmp[0:1, 4, :], row_tmp[0:1, 4, :],
                                                -0.5, 1.5, OP.mult, OP.add))
        B.vop(lambda e: nc.vector.tensor_tensor(row_tmp[0:1, 3, :], row_tmp[0:1, 3, :],
                                                row_tmp[0:1, 4, :], OP.mult))
        vv = B.vop(lambda e: nc.vector.tensor_tensor(row_tmp[0:1, 4, :], row_tmp[0:1, 0, :],
                                                     row_tmp[0:1, 3, :], OP.mult), sem="ve")
        B.wait("tensor", "ve", vv)
        B.pop(lambda e: nc.tensor.matmul(ps_aux[:, 0, :], sb_ones_row[0:1, :],
                                         row_tmp[0:1, 3, :], start=True, stop=True))
        pv = B.pop(lambda e: nc.tensor.matmul(ps_aux[:, 1, :], sb_ones_row[0:1, :],
                                              row_tmp[0:1, 4, :], start=True, stop=True),
                   sem="pe")
        B.wait("vector", "pe", pv)
        B.vop(lambda e: nc.vector.tensor_copy(bc_rs[:, :], ps_aux[:, 0, :]))
        vv = B.vop(lambda e: nc.vector.tensor_copy(bc_mrs[:, :], ps_aux[:, 1, :]), sem="ve")
        state["aux_free"] = vv
        return vv

    def ln_normalize(dst, src):
        vv = None
        for g in range(G):
            B.vop(lambda e, g=g: nc.vector.tensor_tensor(dst[:, g, :], src[:, g, :],
                                                         bc_rs[:, :], OP.mult))
            vv = B.vop(lambda e, g=g: nc.vector.tensor_tensor(dst[:, g, :], dst[:, g, :],
                                                              bc_mrs[:, :], OP.subtract),
                       sem="ve")
        return vv

    with B.arena():
        rx = B.sb("p1_rx", [128, G, CT])
        xln = B.sb("p1_xln", [128, G, CT])
        sb_win = B.sb("p1_win", [128, G, I2])
        wv = B.dma("scalar", sb_win.ap(),
                   w_in.rearrange("(g p) j -> p g j", p=128), sem="w")
        last_xln_use = 0
        rx_free = 0
        mm_bank_free = {0: 0, 1: 0}
        ev_slot_free = {0: 0, 1: 0}
        B.wait("tensor", "w", wv)
        for c in range(NCH):
            B.wait("gpsimd", "ve", rx_free)
            gl = B.dma("gpsimd", rx.ap().rearrange("p g t -> p (g t)"),
                       xT.rearrange("(g p) t -> p g t", p=128)[:, :, c * CT:(c + 1) * CT],
                       sem="g")
            B.wait("scalar", "g", gl)
            B.wait("tensor", "g", gl)
            B.wait("vector", "g", gl)
            ln_stats(c, rx)
            B.wait("vector", "pe", last_xln_use)
            vv = ln_normalize(xln, rx)
            rx_free = vv
            B.wait("tensor", "ve", vv)
            for j in range(I2 // 128):
                bank = j % 2
                B.wait("tensor", "ve", mm_bank_free[bank])
                for k in range(G):
                    mm = B.pop(lambda e, j=j, k=k, bank=bank: nc.tensor.matmul(
                        ps_mm[:, bank, :], sb_win[:, k, j * 128:(j + 1) * 128], xln[:, k, :],
                        start=(k == 0), stop=(k == G - 1)),
                        sem="pe" if k == G - 1 else None)
                B.wait("vector", "pe", mm)
                B.wait("vector", "g", ev_slot_free[bank])
                ev = B.vop(lambda e, j=j, bank=bank: nc.vector.tensor_scalar(
                    evac[:, bank, :], ps_mm[:, bank, :], sb_binc[:, j:j + 1], None,
                    OP.add, OP.bypass), sem="ve")
                mm_bank_free[bank] = ev
                B.wait("gpsimd", "ve", ev)
                gv = B.dma("gpsimd", xz_s[j * 128:(j + 1) * 128, c * CT:(c + 1) * CT],
                           evac[:, bank, :], sem="g")
                ev_slot_free[bank] = gv
            last_xln_use = B.val("pe")
        B.flush()

    if upto == "p1":
        _emit_dump(B, nc, dram["dbg"], xz_s, I2)
        B.st.close()
        return nc, outputs

    with B.arena():
        xch = B.sb("p2_xch", [128, G, CT + 3])
        xsb = B.sb("p2_xs", [128, G, CT])
        dtc = B.sb("p2_dt", [128, G, CT])
        uc = B.sb("p2_u", [128, G, CT])
        yc = B.sb("p2_y", [128, G, CT])
        tmpa = B.sb("p2_tmpa", [128, CT])
        tmpb = B.sb("p2_tmpb", [128, CT])
        tmpc = sq
        tmpd_ap = evac[:, 0, :]
        rx2 = B.sb("p2_rx2", [128, G, CT])
        BmT = B.sb("p2_BmT", [NST, CT])
        CmT = B.sb("p2_CmT", [NST, CT])
        Btok = B.sb("p2_Btok", [128, 4, NST])
        Ctok = B.sb("p2_Ctok", [128, 4, NST])
        Bflat = B.sb("p2_Bflat", [1, 128 * NST])
        Cflat = B.sb("p2_Cflat", [1, 128 * NST])
        sbw = B.sb("p2_w", [128, G, D])
        dtA = B.sb("p2_dtA", [128, 2, G, NST])
        Abar = B.sb("p2_Abar", [128, 2, G, NST])
        bx = B.sb("p2_bx", [128, G, NST])
        sA = B.sb("p2_sA", [128, G, NST])
        stt = B.sb("p2_state", [128, G, NST])
        stp = B.sb("p2_statep", [128, G, NST])

        vv = B.vop(lambda e: nc.vector.memset(stt[:], 0.0), sem="ve")
        ps_cc = ps_mm2
        xzr = xz_s[0:D, :].rearrange("(g p) t -> p g t", p=128)
        zr = xz_s[D:2 * D, :].rearrange("(g p) t -> p g t", p=128)
        w_free_pe = 0
        chunk_bufs_free = 0
        for c in range(NCH):
            lo = c * CT - 2
            hi = c * CT + CT + 1
            vlo, vhi = max(lo, 0), min(hi, T)
            B.wait("gpsimd", "ve", chunk_bufs_free)
            B.wait("vector", "g")
            B.wait("scalar", "g")
            if lo < 0 or hi > T:
                gv0 = B.vop(lambda e: nc.vector.memset(xch[:], 0.0), sem="ve")
                B.wait("gpsimd", "ve", gv0)
                for gg in range(G):
                    gl = B.dma("gpsimd", xch[:, gg, vlo - lo:vhi - lo],
                               xzr[:, gg, vlo:vhi], sem="g")
            else:
                gl = B.dma("gpsimd", xch[:, :, vlo - lo:vhi - lo],
                           xzr[:, :, vlo:vhi], sem="g")
            B.wait("scalar", "pe", w_free_pe)
            wv = B.dma("scalar", sbw.ap(), w_dt.rearrange("(g p) j -> p g j", p=128),
                       sem="w")

            def _p2_dump_and_continue(buf):
                vvx = B.vop(lambda e: nc.vector.tensor_copy(tmpa[:, 0:1], buf[:, 0, 0:1]),
                            sem="ve")
                B.wait("gpsimd", "ve")
                bap = buf.ap() if hasattr(buf, "ap") and not isinstance(buf, bass.AP) else buf
                B.dma("gpsimd", resid_s.rearrange("(g p) t -> p g t", p=128)
                      [:, :, c * CT:(c + 1) * CT], bap, sem="g")
                return B.val("ve")
            B.wait("vector", "g", gl)
            silu_val = {0: 0, 1: 0}
            for g in range(G):
                t0, t1 = (tmpa[:, :], tmpb[:, :]) if g % 2 == 0 else (tmpc[:, :], tmpd_ap)
                B.wait("vector", "sc", silu_val[g % 2])
                B.vop(lambda e, g=g, t0=t0: nc.vector.tensor_scalar(
                    t0, xch[:, g, 0:CT], sb_convw[:, g, 0:1], None,
                    OP.mult, OP.bypass))
                B.vop(lambda e, g=g, t0=t0, t1=t1: nc.vector.scalar_tensor_tensor(
                    t1, xch[:, g, 1:CT + 1], sb_convw[:, g, 1:2], t0,
                    OP.mult, OP.add))
                B.vop(lambda e, g=g, t0=t0, t1=t1: nc.vector.scalar_tensor_tensor(
                    t0, xch[:, g, 2:CT + 2], sb_convw[:, g, 2:3], t1,
                    OP.mult, OP.add))
                cv = B.vop(lambda e, g=g, t0=t0, t1=t1: nc.vector.scalar_tensor_tensor(
                    t1, xch[:, g, 3:CT + 3], sb_convw[:, g, 3:4], t0,
                    OP.mult, OP.add), sem="ve")
                B.wait("scalar", "ve", cv)
                sv = B.sop(lambda e, g=g, t1=t1: nc.scalar.activation(
                    xsb[:, g, :], t1, ACT.Silu, bias=sb_convb[:, g:g + 1]),
                    sem="sc")
                silu_val[g % 2] = sv
            B.wait("tensor", "sc", sv)
            if _P2STOP == "conv":
                B.wait("vector", "sc", sv)
                chunk_bufs_free = _p2_dump_and_continue(xsb)
                continue
            if _P2STOP == "xch":
                B.wait("vector", "sc", sv)
                chunk_bufs_free = _p2_dump_and_continue(xch[:, :, 2:CT + 2])
                continue
            B.wait("tensor", "w", wv)
            B.wait("tensor", "ve", state["aux_free"])
            for k in range(G):
                B.pop(lambda e, k=k: nc.tensor.matmul(
                    ps_aux[0:NST, 0, :], sb_wB[:, k, :], xsb[:, k, :],
                    start=(k == 0), stop=(k == G - 1)))
                pv = B.pop(lambda e, k=k: nc.tensor.matmul(
                    ps_aux[0:NST, 1, :], sb_wC[:, k, :], xsb[:, k, :],
                    start=(k == 0), stop=(k == G - 1)), sem="pe")
            B.wait("vector", "pe", pv)
            B.vop(lambda e: nc.vector.tensor_scalar(
                BmT[:, :], ps_aux[0:NST, 0, :], sb_bB[:, 0:1], None, OP.add, OP.bypass))
            bv = B.vop(lambda e: nc.vector.tensor_scalar(
                CmT[:, :], ps_aux[0:NST, 1, :], sb_bC[:, 0:1], None, OP.add, OP.bypass),
                sem="ve")
            state["aux_free"] = bv
            mm_bank_free = {0: 0, 1: 0}
            for j in range(G):
                bank = j % 2
                B.wait("tensor", "sc", mm_bank_free[bank])
                for k in range(G):
                    mm = B.pop(lambda e, j=j, k=k, bank=bank: nc.tensor.matmul(
                        ps_mm[:, bank, :], sbw[:, k, j * 128:(j + 1) * 128],
                        xsb[:, k, :], start=(k == 0), stop=(k == G - 1)),
                        sem="pe" if k == G - 1 else None)
                B.wait("scalar", "pe", mm)
                B.sop(lambda e, j=j, bank=bank: nc.scalar.activation(
                    evac[:, bank, :], ps_mm[:, bank, :], ACT.Exp,
                    bias=sb_bdt[:, j:j + 1]))
                ev = B.sop(lambda e, j=j, bank=bank: nc.scalar.activation(
                    evac[:, bank, :], evac[:, bank, :], ACT.Ln, bias=1.0), sem="sc")
                mm_bank_free[bank] = ev
                B.wait("vector", "sc", ev)
                B.vop(lambda e, j=j, bank=bank: nc.vector.tensor_scalar(
                    dtc[:, j, :], evac[:, bank, :], 0.2, 0.001, OP.min, OP.max))
                uv = B.vop(lambda e, j=j: nc.vector.tensor_tensor(
                    uc[:, j, :], dtc[:, j, :], xsb[:, j, :], OP.mult), sem="ve")
            if _P2STOP == "dtu":
                chunk_bufs_free = _p2_dump_and_continue(dtc)
                continue
            B.wait("gpsimd", "ve", uv)
            zl = B.dma("gpsimd", xsb.ap().rearrange("p g t -> p (g t)"),
                       zr[:, :, c * CT:(c + 1) * CT], sem="g")
            rl = B.dma("gpsimd", rx2.ap().rearrange("p g t -> p (g t)"),
                       xT.rearrange("(g p) t -> p g t", p=128)[:, :, c * CT:(c + 1) * CT],
                       sem="g")
            B.wait("scalar", "pe", mm)
            wv2 = B.dma("scalar", sbw.ap(), w_out.rearrange("(g p) j -> p g j", p=128),
                        sem="w")
            B.wait("tensor", "ve", bv)
            tv = 0
            for sub in range(4):
                B.wait("tensor", "ve", tv)
                B.pop(lambda e, sub=sub: nc.tensor.transpose(
                    ps_bc[0:128, 0, 0:NST], BmT[:, sub * 128:(sub + 1) * 128],
                    sb_eye[0:NST, 0:NST]))
                pv = B.pop(lambda e, sub=sub: nc.tensor.transpose(
                    ps_bc[0:128, 1, 0:NST], CmT[:, sub * 128:(sub + 1) * 128],
                    sb_eye[0:NST, 0:NST]), sem="pe")
                B.wait("vector", "pe", pv)
                B.vop(lambda e, sub=sub: nc.vector.tensor_copy(
                    Btok[:, sub, :], ps_bc[0:128, 0, 0:NST]))
                tv = B.vop(lambda e, sub=sub: nc.vector.tensor_copy(
                    Ctok[:, sub, :], ps_bc[0:128, 1, 0:NST]), sem="ve")
            ve0 = B.val("ve")
            sc0 = B.val("sc")
            pe0 = B.val("pe")
            g0 = B.val("g")
            for sub in range(4):
                B.wait("gpsimd", "ve", tv)
                if sub > 0:
                    B.wait("gpsimd", "pe", pe0 + sub * 8)
                B.dma("gpsimd", Bflat.ap().rearrange("one (p n) -> one p n", n=NST),
                      Btok[:, sub, :], sem="g")
                gsv = B.dma("gpsimd", Cflat.ap().rearrange("one (p n) -> one p n", n=NST),
                            Ctok[:, sub, :], sem="g")
                for grp in range(8):
                    bank = grp % 2
                    ag = sub * 8 + grp
                    t0a = ag * 16
                    B.wait("tensor", "g", gsv)
                    if ag >= 2:
                        B.wait("tensor", "ve", ve0 + t0a - 14)
                    for s in range(16):
                        tl = (t0a + s) % 128
                        B.pop(lambda e, s=s, tl=tl, bank=bank: nc.tensor.matmul(
                            ps_bc[:, bank, s * NST:(s + 1) * NST],
                            sb_ones_row[0:1, :], Bflat[0:1, tl * NST:(tl + 1) * NST],
                            start=True, stop=True))
                        B.pop(lambda e, s=s, tl=tl, bank=bank: nc.tensor.matmul(
                            ps_cc[:, bank, s * NST:(s + 1) * NST],
                            sb_ones_row[0:1, :], Cflat[0:1, tl * NST:(tl + 1) * NST],
                            start=True, stop=True),
                            sem="pe" if s == 15 else None)
            B.vop(lambda e: nc.vector.tensor_tensor(
                dtA[:, 0], sb_A[:], dtc[:, :, 0:1].broadcast_to([128, G, NST]),
                OP.mult), sem="ve")
            for t in range(CT):
                grp, s = t // 16, t % 16
                bank = grp % 2
                if t + 1 < CT:
                    B.vop(lambda e, t=t: nc.vector.tensor_tensor(
                        dtA[:, (t + 1) % 2], sb_A[:],
                        dtc[:, :, t + 1:t + 2].broadcast_to([128, G, NST]), OP.mult),
                        sem="ve")
                B.wait("scalar", "ve", ve0 + t + 1)
                sv = B.sop(lambda e, t=t: nc.scalar.activation(
                    Abar[:, t % 2], dtA[:, t % 2], ACT.Exp), sem="sc")
                B.wait("vector", "sc", sv)
                B.wait("vector", "pe", pe0 + grp + 1)
                B.vop(lambda e, t=t, bank=bank, s=s: nc.vector.tensor_tensor(
                    bx[:], uc[:, :, t:t + 1].broadcast_to([128, G, NST]),
                    ps_bc[:, bank:bank + 1, s * NST:(s + 1) * NST]
                    .broadcast_to([128, G, NST]), OP.mult))
                B.vop(lambda e, t=t: nc.vector.tensor_tensor(
                    sA[:], stt[:], Abar[:, t % 2], OP.mult))
                B.vop(lambda e: nc.vector.tensor_tensor(stp[:], sA[:], bx[:], OP.add))
                B.vop(lambda e: nc.vector.tensor_scalar(
                    stt[:], stp[:], 10.0, -10.0, OP.min, OP.max))
                B.vop(lambda e, bank=bank, s=s: nc.vector.tensor_tensor(
                    stp[:], stt[:],
                    ps_cc[:, bank:bank + 1, s * NST:(s + 1) * NST]
                    .broadcast_to([128, G, NST]), OP.mult))
                B.vop(lambda e, t=t: nc.vector.tensor_reduce(
                    yc[:, :, t:t + 1], stp[:], AX, OP.add))
            scan_done = B.vop(lambda e: nc.vector.tensor_copy(tmpa[:, 0:1], stt[:, 0, 0:1]),
                              sem="ve")
            if _P2STOP == "scan":
                chunk_bufs_free = _p2_dump_and_continue(yc)
                continue
            B.wait("scalar", "g", zl)
            gate_prev = 0
            for g in range(G):
                B.wait("scalar", "ve", gate_prev)
                sv = B.sop(lambda e, g=g: nc.scalar.activation(
                    sq[:, :], xsb[:, g, :], ACT.Silu), sem="sc")
                B.wait("vector", "sc", sv)
                gv2 = B.vop(lambda e, g=g: nc.vector.tensor_tensor(
                    yc[:, g, :], yc[:, g, :], sq[:, :], OP.mult), sem="ve")
                gate_prev = gv2
            if _P2STOP == "gate":
                chunk_bufs_free = _p2_dump_and_continue(yc)
                continue
            B.wait("tensor", "ve", gv2)
            B.wait("tensor", "w", wv2)
            mmof = {0: 0, 1: 0}
            for j in range(G):
                bank = j % 2
                B.wait("tensor", "ve", mmof[bank])
                for k in range(G):
                    mm = B.pop(lambda e, j=j, k=k, bank=bank: nc.tensor.matmul(
                        ps_mm[:, bank, :], sbw[:, k, j * 128:(j + 1) * 128],
                        yc[:, k, :], start=(k == 0), stop=(k == G - 1)),
                        sem="pe" if k == G - 1 else None)
                B.wait("vector", "pe", mm)
                ov = B.vop(lambda e, j=j, bank=bank: nc.vector.tensor_scalar(
                    dtc[:, j, :], ps_mm[:, bank, :], sb_bout[:, j:j + 1], None,
                    OP.add, OP.bypass), sem="ve")
                mmof[bank] = ov
            w_free_pe = B.val("pe")
            B.wait("vector", "g", rl)
            B.wait("scalar", "g", rl)
            B.wait("tensor", "g", rl)
            ln_stats(c, rx2)
            for g in range(G):
                B.vop(lambda e, g=g: nc.vector.tensor_tensor(
                    uc[:, g, :], rx2[:, g, :], bc_rs[:, :], OP.mult))
                B.vop(lambda e, g=g: nc.vector.tensor_tensor(
                    uc[:, g, :], uc[:, g, :], bc_mrs[:, :], OP.subtract))
                vv = B.vop(lambda e, g=g: nc.vector.tensor_tensor(
                    dtc[:, g, :], dtc[:, g, :], uc[:, g, :], OP.add), sem="ve")
            ln_stats(c, dtc)
            for g in range(G):
                B.vop(lambda e, g=g: nc.vector.tensor_tensor(
                    uc[:, g, :], dtc[:, g, :], bc_rs[:, :], OP.mult))
                B.vop(lambda e, g=g: nc.vector.tensor_tensor(
                    uc[:, g, :], uc[:, g, :], bc_mrs[:, :], OP.subtract))
                B.vop(lambda e, g=g: nc.vector.tensor_scalar(
                    uc[:, g, :], uc[:, g, :], sb_lng[:, g:g + 1], sb_lnb[:, g:g + 1],
                    OP.mult, OP.add))
                rv = B.vop(lambda e, g=g: nc.vector.tensor_tensor(
                    uc[:, g, :], uc[:, g, :], rx2[:, g, :], OP.add), sem="ve")
                B.wait("gpsimd", "ve", rv)
                B.dma("gpsimd", resid_s.rearrange("(g p) t -> p g t", p=128)
                      [:, g, c * CT:(c + 1) * CT], uc[:, g, :], sem="g")
            chunk_bufs_free = B.val("ve")
        B.flush()

    if upto == "p2":
        _emit_dump(B, nc, dram["dbg"], resid_s, D)
        B.st.close()
        return nc, outputs

    PSBANKS = (ps_mm, ps_mm2, ps_aux, ps_bc)

    def psbank(i):
        return PSBANKS[i // 2][:, i % 2, :]

    def emit_ffn(out_dst, tag):
        routing_arena = B.arena()
        routing_arena.__enter__()
        rt = B.sb("rt" + tag, [128, 32, 40])
        mrows0 = B.sb("mr0" + tag, [8, T])
        mrows1 = B.sb("mr1" + tag, [8, T])
        xtok_r = xtok_s
        resid_r = resid_s.rearrange("(g p) t -> p g t", p=128)
        xln_r = xln_s.rearrange("(g p) t -> p g t", p=128)
        y_r = y_s.rearrange("(g p) t -> p g t", p=128)
        g_r = g_s.rearrange("(g p) t -> p g t", p=128)
        with B.arena():
            rx = B.sb("fa_rx" + tag, [128, G, CT])
            xln = B.sb("fa_xln" + tag, [128, G, CT])
            swj = B.sb("fa_swj" + tag, [128, 2, G, CT])
            tmpa = B.sb("fa_tmpa" + tag, [128, CT])
            tmpb = B.sb("fa_tmpb" + tag, [128, CT])
            xtk = B.sb("fa_xtk" + tag, [128, 2, 1024])
            lgall = B.sb("fa_lgall" + tag, [128, 32])
            mtmp = B.sb("fa_mtmp" + tag, [128, 32])
            cols = B.sb("fa_cols" + tag, [128, 16])
            srow = B.sb("fa_srow" + tag, [4, 128])
            rx_free = 0
            swj_free = 0
            ev_free = {0: 0, 1: 0}
            for c in range(NCH):
                B.wait("gpsimd", "ve", rx_free)
                B.wait("vector", "g")
                B.wait("scalar", "g")
                gl = B.dma("gpsimd", rx.ap(),
                           resid_r[:, :, c * CT:(c + 1) * CT], sem="g")
                B.wait("scalar", "g", gl)
                B.wait("tensor", "g", gl)
                B.wait("vector", "g", gl)
                ln_stats(c, rx)
                vv = ln_normalize(xln, rx)
                rx_free = vv
                B.wait("tensor", "ve", vv)
                xtk_free = state.get("xtk_free", 0)
                for sub in range(4):
                    xslot = sub % 2
                    for g in range(G):
                        bank = g % 2
                        B.wait("tensor", "ve", state.get(("psbc", bank), 0))
                        pv = B.pop(lambda e, g=g, sub=sub, bank=bank:
                                   nc.tensor.transpose(
                            ps_bc[:, bank, 0:128],
                            xln[:, g, sub * 128:(sub + 1) * 128], sb_eye[:, :]),
                            sem="pe")
                        B.wait("vector", "pe", pv)
                        if g == 0:
                            B.wait("vector", "g", state.get(("xts", xslot), 0))
                        tv2 = B.vop(lambda e, g=g, xslot=xslot, bank=bank:
                                    nc.vector.tensor_copy(
                            xtk[:, xslot, g * 128:(g + 1) * 128],
                            ps_bc[:, bank, 0:128]), sem="ve")
                        state[("psbc", bank)] = tv2
                    B.wait("gpsimd", "ve", tv2)
                    gv2 = B.dma("gpsimd",
                                xtok_r[c * CT + sub * 128:c * CT + (sub + 1) * 128, :],
                                xtk[:, xslot, :], sem="g")
                    state[("xts", xslot)] = gv2
                def bounce():
                    vv_ = B.vop(lambda e: nc.vector.engine_nop(), sem="ve")
                    B.wait("scalar", "ve", vv_)
                    inc = B.tick("sc")
                    B.emit("scalar", lambda e, inc=inc: None if e.sem_inc(inc[0], inc[1]) else None)
                    B.wait("vector", "sc")

                B.wait("tensor", "ve", vv)
                exp_done = {0: 0, 1: 0}
                for sub in range(4):
                    bank = sub % 2
                    B.wait("tensor", "sc", exp_done[bank])
                    B.wait("tensor", "ve", state["aux_free"])
                    for g in range(G):
                        pv = B.pop(lambda e, g=g, sub=sub, bank=bank: nc.tensor.matmul(
                            ps_aux[:, bank, 0:8], xln[:, g, sub * 128:(sub + 1) * 128],
                            sb_gatew[:, g, :], start=(g == 0), stop=(g == G - 1)),
                            sem="pe" if g == G - 1 else None)
                    B.wait("scalar", "pe", pv)
                    sv = B.sop(lambda e, sub=sub, bank=bank: nc.scalar.activation(
                        lgall[:, sub * 8:(sub + 1) * 8], ps_aux[:, bank, 0:8],
                        ACT.Exp), sem="sc")
                    exp_done[bank] = sv
                state["aux_free"] = B.val("ve")
                B.wait("vector", "sc", sv)
                for sub in range(4):
                    B.vop(lambda e, sub=sub: nc.vector.reduce_max(
                        cols[:, sub:sub + 1], lgall[:, sub * 8:(sub + 1) * 8], AX))
                    B.vop(lambda e, sub=sub: nc.vector.reduce_sum(
                        cols[:, 4 + sub:5 + sub], lgall[:, sub * 8:(sub + 1) * 8], AX))
                bounce()
                B.wait("tensor", "ve")
                B.wait("tensor", "sc")
                pv = B.pop(lambda e: nc.tensor.transpose(
                    ps_aux[0:4, 0, 0:128], cols[:, 4:8], sb_eye[:, :]), sem="pe")
                B.wait("vector", "pe", pv)
                vv2 = B.vop(lambda e: nc.vector.tensor_copy(
                    srow[0:4, :], ps_aux[0:4, 0, 0:128]), sem="ve")
                bounce()
                B.vop(lambda e: nc.vector.reciprocal(srow[0:4, :], srow[0:4, :]))
                bounce()
                B.wait("tensor", "ve")
                pv = B.pop(lambda e: nc.tensor.transpose(
                    ps_aux[0:128, 1, 0:4], srow[0:4, :], sb_eye[0:4, 0:4]), sem="pe")
                B.wait("vector", "pe", pv)
                B.vop(lambda e: nc.vector.tensor_copy(
                    cols[:, 12:16], ps_aux[0:128, 1, 0:4]))
                for sub in range(4):
                    si = c * 4 + sub
                    B.vop(lambda e, si=si, sub=sub: nc.vector.tensor_scalar(
                        rt[:, si, 16:24], lgall[:, sub * 8:(sub + 1) * 8],
                        cols[:, sub:sub + 1], None, OP.is_equal, OP.bypass))
                bounce()
                for sub in range(4):
                    si = c * 4 + sub
                    B.vop(lambda e, si=si, sub=sub: nc.vector.scalar_tensor_tensor(
                        mtmp[:, sub * 8:(sub + 1) * 8], rt[:, si, 16:24], -1e30,
                        lgall[:, sub * 8:(sub + 1) * 8], OP.mult, OP.add))
                bounce()
                for sub in range(4):
                    B.vop(lambda e, sub=sub: nc.vector.reduce_max(
                        cols[:, 8 + sub:9 + sub], mtmp[:, sub * 8:(sub + 1) * 8], AX))
                bounce()
                for sub in range(4):
                    si = c * 4 + sub
                    B.vop(lambda e, si=si, sub=sub: nc.vector.tensor_scalar(
                        rt[:, si, 24:32], mtmp[:, sub * 8:(sub + 1) * 8],
                        cols[:, 8 + sub:9 + sub], None, OP.is_equal, OP.bypass))
                    B.vop(lambda e, si=si, sub=sub: nc.vector.tensor_scalar(
                        rt[:, si, 32:33], cols[:, sub:sub + 1],
                        cols[:, 12 + sub:13 + sub], None, OP.mult, OP.bypass))
                    B.vop(lambda e, si=si, sub=sub: nc.vector.tensor_scalar(
                        rt[:, si, 33:34], cols[:, 8 + sub:9 + sub],
                        cols[:, 12 + sub:13 + sub], None, OP.mult, OP.bypass))
                bounce()
                B.wait("tensor", "ve")
                for sub in range(4):
                    si = c * 4 + sub
                    for half in (0, 1):
                        bank = half % 2
                        mdst = mrows0 if half == 0 else mrows1
                        B.wait("tensor", "ve", state.get(("psbc", bank), 0))
                        pv = B.pop(lambda e, si=si, half=half, bank=bank:
                                   nc.tensor.transpose(
                            ps_bc[0:8, bank, 0:128],
                            rt[:, si, 16 + half * 8:24 + half * 8],
                            sb_eye[:, :]), sem="pe")
                        B.wait("vector", "pe", pv)
                        rv = B.vop(lambda e, si=si, mdst=mdst, bank=bank:
                                   nc.vector.tensor_copy(
                            mdst[0:8, si * 128:(si + 1) * 128],
                            ps_bc[0:8, bank, 0:128]), sem="ve")
                        state[("psbc", bank)] = rv
                state["aux_free"] = B.val("ve")
                sv_prev, ev_prev = 0, 0
                for jg in range(4):
                    B.wait("scalar", "pe", swj_free)
                    B.dma("scalar", swj[:, 0, :, :],
                          sw1.rearrange("(g p) j -> p g j", p=128)
                          [:, :, jg * 512:(jg + 1) * 512], sem="w")
                    wl = B.dma("scalar", swj[:, 1, :, :],
                               sw2.rearrange("(g p) j -> p g j", p=128)
                               [:, :, jg * 512:(jg + 1) * 512], sem="w")
                    B.wait("tensor", "w", wl)
                    for jj in range(4):
                        j = jg * 4 + jj
                        tmp = tmpa if jj % 2 == 0 else tmpb
                        B.wait("tensor", "sc", sv_prev)
                        for k in range(G):
                            mm = B.pop(lambda e, jj=jj, k=k: nc.tensor.matmul(
                                ps_mm[:, 0, :], swj[:, 0, k, jj * 128:(jj + 1) * 128],
                                xln[:, k, :], start=(k == 0), stop=(k == G - 1)),
                                sem="pe" if k == G - 1 else None)
                        B.wait("scalar", "pe", mm)
                        sv = B.sop(lambda e, tmp=tmp: nc.scalar.activation(
                            tmp[:, :], ps_mm[:, 0, :], ACT.Silu), sem="sc")
                        sv_prev = sv
                        B.wait("tensor", "ve", ev_prev)
                        for k in range(G):
                            mm = B.pop(lambda e, jj=jj, k=k: nc.tensor.matmul(
                                ps_mm[:, 1, :], swj[:, 1, k, jj * 128:(jj + 1) * 128],
                                xln[:, k, :], start=(k == 0), stop=(k == G - 1)),
                                sem="pe" if k == G - 1 else None)
                        B.wait("vector", "pe", mm)
                        B.wait("vector", "sc", sv)
                        eslot = jj % 2
                        B.wait("vector", "g", ev_free[eslot])
                        ev = B.vop(lambda e, tmp=tmp, eslot=eslot: nc.vector.tensor_tensor(
                            evac[:, eslot, :], tmp[:, :], ps_mm[:, 1, :], OP.mult),
                            sem="ve")
                        ev_prev = ev
                        B.wait("gpsimd", "ve", ev)
                        gv = B.dma("gpsimd", g_s[j * 128:(j + 1) * 128,
                                               c * CT:(c + 1) * CT],
                                   evac[:, eslot, :], sem="g")
                        ev_free[eslot] = gv
                    swj_free = B.val("pe")
            B.flush()
        if _P3STOP == "rt":
            vvx = B.vop(lambda e: nc.vector.tensor_copy(
                evac[:, 0, 0:1], rt[:, 0, 0:1]), sem="ve")
            B.wait("gpsimd", "ve", vvx)
            B.dma("gpsimd", y_s[0:128, 0:1280],
                  rt.ap().rearrange("p a b -> p (a b)"), sem="g")
            B.wait("gpsimd", "g")
            B.flush()
            _emit_dump(B, nc, dram["dbg"], y_s, D)
            return "stop"
        if _P3STOP == "xtok":
            _emit_dump(B, nc, dram["dbg"], xtok_s[0:D, 0:D], D, cols=D)
            return "stop"
        if _P3STOP in ("xln", "gs0", "gs1"):
            src = {"xln": xln_s, "gs0": g_s[0:D, :], "gs1": g_s[D:2 * D, :]}[_P3STOP]
            _emit_dump(B, nc, dram["dbg"], src, D)
            return "stop"
        with B.arena():
            cums0 = B.sb("fi_cums0" + tag, [8, T])
            cums1 = B.sb("fi_cums1" + tag, [8, T])
            prodr0 = B.sb("fi_prod0" + tag, [8, T])
            prodr1 = B.sb("fi_prod1" + tag, [8, T])
            ones8 = B.sb("fi_ones" + tag, [8, CT])
            cnt8 = B.sb("fi_cnt8" + tag, [1, 8])
            cntc = B.sb("fi_cntc" + tag, [8, 1])
            ebcol = B.sb("fi_eb" + tag, [40, 1])
            slotr0 = B.sb("fi_slotr0" + tag, [1, T])
            slotr1 = B.sb("fi_slotr1" + tag, [1, T])
            scol = B.sb("fi_scol" + tag, [128, 2])
            scoli = B.sb("fi_scoli" + tag, [128, 2], mybir.dt.int32)
            tki = B.sb("fi_tki" + tag, [128, 32], mybir.dt.int32)
            wcol = B.sb("fi_wcol" + tag, [128, 1])
            zrow = B.sb("fi_zrow" + tag, [128, D])
            zti = B.sb("fi_zti" + tag, [128, 32], mybir.dt.int32)

            def bounce3():
                vv_ = B.vop(lambda e: nc.vector.engine_nop(), sem="ve")
                B.wait("scalar", "ve", vv_)
                inc = B.tick("sc")
                B.emit("scalar", lambda e, inc=inc: e.sem_inc(inc[0], inc[1]))
                B.wait("vector", "sc")

            wv = B.dma("scalar", ebcol[:, :], ebase, sem="w")
            B.dma("scalar", tki[:, :], dram["tokid"], sem="w")
            B.wait("vector", "w", wv)
            B.vop(lambda e: nc.vector.memset(ones8[:, :], 1.0))
            B.vop(lambda e: nc.vector.memset(zrow[:, :], 0.0))
            vv = B.vop(lambda e: nc.vector.memset(zti[:, :], T), sem="ve")
            B.wait("gpsimd", "ve", vv)
            for r in range(0, T + 128, 128):
                B.dma("gpsimd", ytok_s[r:r + 128, :], zrow[:, :], sem="g")
            B.dma("gpsimd", xtok_s[T:T + 128, :], zrow[:, :], sem="g")
            for r in range(0, TOTSLOT, 128):
                n = min(128, TOTSLOT - r)
                B.dma("gpsimd", tok_s[r:r + n, :], zti[0:n, 0:1], sem="g")
                B.dma("gpsimd", wslot_s[r:r + n, :],
                      zrow[0:n, 0:1], sem="g")
            if _P3STOP == "fidx0":
                B.flush()
                return "stopdump"
            B.wait("vector", "g")
            for cb in range(8):
                B.vop(lambda e, cb=cb: nc.vector.tensor_tensor_scan(
                    cums0[:, cb * CT:(cb + 1) * CT], ones8[:, :],
                    mrows0[:, cb * CT:(cb + 1) * CT],
                    0.0 if cb == 0 else cums0[:, cb * CT - 1:cb * CT],
                    OP.mult, OP.add))
                vv = B.vop(lambda e, cb=cb: nc.vector.tensor_tensor_scan(
                    cums1[:, cb * CT:(cb + 1) * CT], ones8[:, :],
                    mrows1[:, cb * CT:(cb + 1) * CT],
                    0.0 if cb == 0 else cums1[:, cb * CT - 1:cb * CT],
                    OP.mult, OP.add), sem="ve")
                bounce3()
            if _P3STOP == "fidx1":
                B.flush()
                return "stopdump"
            B.wait("tensor", "ve", vv)
            pv = B.pop(lambda e: nc.tensor.transpose(
                ps_aux[0:128, 0, 0:8], cums0[:, T - 128:T], sb_eye[0:8, 0:8]),
                sem="pe")
            B.wait("vector", "pe", pv)
            vv = B.vop(lambda e: nc.vector.tensor_copy(
                evac[:, 0, 0:8], ps_aux[0:128, 0, 0:8]), sem="ve")
            B.wait("gpsimd", "ve", vv)
            gv = B.dma("gpsimd", cnt8[0:1, :],
                       evac.ap().rearrange("p b t -> p (b t)")[127:128, 0:8], sem="g")
            B.wait("tensor", "g", gv)
            pv = B.pop(lambda e: nc.tensor.transpose(
                ps_aux[0:8, 1, 0:1], cnt8[0:1, 0:8], sb_eye[0:1, 0:1]), sem="pe")
            B.wait("vector", "pe", pv)
            vv = B.vop(lambda e: nc.vector.tensor_copy(
                cntc[0:8, 0:1], ps_aux[0:8, 1, 0:1]), sem="ve")
            bounce3()
            if _P3STOP == "fidx1a":
                B.flush()
                return "stopdump"
            B.vop(lambda e: nc.vector.tensor_scalar(
                cums0[:, :], cums0[:, :], ebcol[0:8, 0:1], None, OP.add, OP.bypass))
            B.vop(lambda e: nc.vector.tensor_scalar(
                cums1[:, :], cums1[:, :], ebcol[0:8, 0:1], cntc[0:8, 0:1],
                OP.add, OP.add))
            if _P3STOP == "fidx1b":
                B.flush()
                return "stopdump"
            B.vop(lambda e: nc.vector.tensor_tensor(
                prodr0[:, :], mrows0[:, :], cums0[:, :], OP.mult))
            vv = B.vop(lambda e: nc.vector.tensor_tensor(
                prodr1[:, :], mrows1[:, :], cums1[:, :], OP.mult), sem="ve")
            if _P3STOP == "fidx1c":
                B.flush()
                return "stopdump"
            B.wait("tensor", "ve", vv)
            for cb in range(8):
                B.wait("tensor", "ve", state["aux_free"])
                B.pop(lambda e, cb=cb: nc.tensor.matmul(
                    ps_aux[0:1, 0, :], sb_ones_col[0:8, 0:1],
                    prodr0[0:8, cb * CT:(cb + 1) * CT], start=True, stop=True))
                pv = B.pop(lambda e, cb=cb: nc.tensor.matmul(
                    ps_aux[0:1, 1, :], sb_ones_col[0:8, 0:1],
                    prodr1[0:8, cb * CT:(cb + 1) * CT], start=True, stop=True),
                    sem="pe")
                B.wait("vector", "pe", pv)
                B.vop(lambda e, cb=cb: nc.vector.tensor_copy(
                    slotr0[0:1, cb * CT:(cb + 1) * CT], ps_aux[0:1, 0, :]))
                vv = B.vop(lambda e, cb=cb: nc.vector.tensor_copy(
                    slotr1[0:1, cb * CT:(cb + 1) * CT], ps_aux[0:1, 1, :]), sem="ve")
                state["aux_free"] = vv
            if _P3STOP == "fidx2":
                B.flush()
                return "stopdump"
            B.wait("gpsimd", "g")
            sc_free = 0
            for si in range(32):
                B.wait("tensor", "ve", state["aux_free"] if si == 0 else tv)
                B.pop(lambda e, si=si: nc.tensor.transpose(
                    ps_aux[0:128, 0, 0:1], slotr0[:, si * 128:(si + 1) * 128],
                    sb_eye[0:1, 0:1]))
                pv = B.pop(lambda e, si=si: nc.tensor.transpose(
                    ps_aux[0:128, 1, 0:1], slotr1[:, si * 128:(si + 1) * 128],
                    sb_eye[0:1, 0:1]), sem="pe")
                B.wait("vector", "pe", pv)
                B.wait("vector", "g", sc_free)
                B.vop(lambda e: nc.vector.tensor_copy(
                    scoli[:, 0:1], ps_aux[0:128, 0, 0:1]))
                tv = B.vop(lambda e: nc.vector.tensor_copy(
                    scoli[:, 1:2], ps_aux[0:128, 1, 0:1]), sem="ve")
                B.wait("gpsimd", "ve", tv)
                for k in range(2):
                    B.emit("gpsimd", lambda e, si=si, k=k: e.indirect_dma_start(
                        tok_s, IndirectOffsetOnAxis(scoli[:, k:k + 1], 0),
                        tki[:, si:si + 1], None), inc=B.tick("g", 16))
                    B.emit("gpsimd", lambda e, si=si, k=k: e.indirect_dma_start(
                        wslot_s, IndirectOffsetOnAxis(scoli[:, k:k + 1], 0),
                        rt[:, si, 32 + k:33 + k], None), inc=B.tick("g", 16))
                sc_free = B.val("g")
            if _P3STOP == "fidx3":
                B.flush()
                return "stopdump"
            if _P3STOP == "slots":
                sl_i = B.sb("dd_i" + tag, [128, 1], mybir.dt.int32)
                sl_f = B.sb("dd_f" + tag, [128, 1])
                B.wait("gpsimd", "g")
                for r in range(0, TOTSLOT - 128, 128):
                    lv = B.dma("gpsimd", sl_i[:, :], tok_s[r:r + 128, :], sem="g")
                    B.wait("vector", "g", lv)
                    cv2 = B.vop(lambda e: nc.vector.tensor_copy(sl_f[:, :], sl_i[:, :]),
                                sem="ve")
                    B.wait("gpsimd", "ve", cv2)
                    inc = B.tick("g", 16)
                    B.emit("gpsimd", lambda e, r=r: e.dma_start(
                        ytok_s[0:128, r // 128:r // 128 + 1], sl_f[:, :],
                        allow_slow_non_contiguous=True), inc=inc)
                    lv = B.dma("gpsimd", sl_f[:, :], wslot_s[r:r + 128, :], sem="g")
                    B.wait("gpsimd", "g", lv)
                    inc = B.tick("g", 16)
                    B.emit("gpsimd", lambda e, r=r: e.dma_start(
                        ytok_s[128:256, r // 128:r // 128 + 1], sl_f[:, :],
                        allow_slow_non_contiguous=True), inc=inc)
                B.wait("gpsimd", "ve")
                for q in range(4):
                    B.dma("gpsimd", ytok_s[256 + q:257 + q, 0:D],
                          slotr0[0:1, q * D:(q + 1) * D], sem="g")
                    B.dma("gpsimd", ytok_s[260 + q:261 + q, 0:D],
                          slotr1[0:1, q * D:(q + 1) * D], sem="g")
                for q in range(4):
                    B.dma("gpsimd", ytok_s[264 + q * 8:272 + q * 8, 0:D],
                          cums0[0:8, q * D:(q + 1) * D], sem="g")
                    B.dma("gpsimd", ytok_s[296 + q * 8:304 + q * 8, 0:D],
                          prodr0[0:8, q * D:(q + 1) * D], sem="g")
                    B.dma("gpsimd", ytok_s[328 + q * 8:336 + q * 8, 0:D],
                          mrows0[0:8, q * D:(q + 1) * D], sem="g")
                B.wait("gpsimd", "g")
                B.flush()
                return "stopdump_y"
            B.flush()
        routing_arena.__exit__(None, None, None)

        with B.arena():
            w1b = B.sb("fb_w1" + tag, [128, G, D])
            w2b = B.sb("fb_w2" + tag, [128, G, D])
            w3b = B.sb("fb_w3" + tag, [128, G, D])
            gxt = B.sb("fb_gx" + tag, [128, 2, D])
            gfT = B.sb("fb_gf" + tag, [128, G, CT])
            hhb = B.sb("fb_hh" + tag, [128, G, CT])
            x2sb = B.sb("fb_x2" + tag, [128, G, CT])
            H3sb = B.sb("fb_H3" + tag, [128, G, CT])
            ytk = B.sb("fb_ytk" + tag, [128, 2, D])
            tcol = B.sb("fb_tcol" + tag, [128, 4], mybir.dt.int32)
            wchc = B.sb("fb_wch" + tag, [128, 4])
            sb_b1 = B.sb("fb_b1" + tag, [128, G])
            sb_b2 = B.sb("fb_b2" + tag, [128, G])
            sb_b3 = B.sb("fb_b3" + tag, [128, G])
            w_free = 0
            ysc_free = 0
            gx_free = {0: 0, 1: 0}
            ytk_free = {0: 0, 1: 0}
            for ch in range(NCHS):
                ei = ch * CT // CAPE
                if ch * CT % CAPE == 0:
                    B.wait("scalar", "pe", w_free)
                    B.dma("scalar", w1b.ap(),
                          w1[ei].rearrange("(g p) j -> p g j", p=128), sem="w")
                    B.dma("scalar", w2b.ap(),
                          w2[ei].rearrange("(g p) j -> p g j", p=128), sem="w")
                    B.dma("scalar", w3b.ap(),
                          w3[ei].rearrange("(g p) j -> p g j", p=128), sem="w")
                    B.dma("scalar", sb_b1[:, :], b1[ei], sem="w")
                    B.dma("scalar", sb_b2[:, :], b2[ei], sem="w")
                    wl = B.dma("scalar", sb_b3[:, :], b3[ei], sem="w")
                    B.wait("tensor", "w", wl)
                    B.wait("scalar", "w", wl)
                B.wait("gpsimd", "ve")
                for sub in range(4):
                    B.dma("gpsimd", tcol[:, sub:sub + 1],
                          tok_s[ch * CT + sub * 128:ch * CT + (sub + 1) * 128, :],
                          sem="g")
                    gl = B.dma("gpsimd", wchc[:, sub:sub + 1],
                               wslot_s[ch * CT + sub * 128:ch * CT + (sub + 1) * 128, :],
                               sem="g")
                for sub in range(4):
                    gslot = sub % 2
                    B.wait("gpsimd", "g", gl)
                    B.wait("gpsimd", "pe", gx_free[gslot])
                    inc = B.tick("g", 16)
                    B.emit("gpsimd", lambda e, sub=sub, gslot=gslot:
                           e.indirect_dma_start(
                        gxt[:, gslot, :], None, xtok_s,
                        IndirectOffsetOnAxis(tcol[:, sub:sub + 1], 0)),
                        inc=inc)
                    gv = B.val("g")
                    B.wait("vector", "g", gv)
                    sv = B.vop(lambda e, sub=sub, gslot=gslot:
                               nc.vector.tensor_scalar(
                        gxt[:, gslot, :], gxt[:, gslot, :], wchc[:, sub:sub + 1],
                        None, OP.mult, OP.bypass), sem="ve")
                    B.wait("tensor", "ve", sv)
                    for g in range(G):
                        bank = g % 2
                        B.wait("tensor", "ve", state.get(("psbc", bank), 0))
                        pv = B.pop(lambda e, g=g, gslot=gslot, bank=bank:
                                   nc.tensor.transpose(
                            ps_bc[:, bank, 0:128], gxt[:, gslot, g * 128:(g + 1) * 128],
                            sb_eye[:, :]), sem="pe")
                        B.wait("vector", "pe", pv)
                        tv = B.vop(lambda e, g=g, sub=sub, bank=bank:
                                   nc.vector.tensor_copy(
                            gfT[:, g, sub * 128:(sub + 1) * 128],
                            ps_bc[:, bank, 0:128]), sem="ve")
                        state[("psbc", bank)] = tv
                    gx_free[gslot] = pv
                if _P3STOP == "fbg" and ch == 0:
                    B.wait("gpsimd", "ve")
                    for g in range(G):
                        B.dma("gpsimd", ytok_s[g * 128:(g + 1) * 128, 0:CT],
                              gfT[:, g, :], sem="g")
                    B.wait("gpsimd", "g")
                    B.flush()
                    return "stopdump_y"
                B.wait("tensor", "ve", tv)
                bank_free = {0: 0, 1: 0}
                hfree = {0: 0, 1: 0}
                for j in range(G):
                    bank = j % 2
                    B.wait("tensor", "sc", hfree[bank])
                    for k in range(G):
                        mm = B.pop(lambda e, j=j, k=k, bank=bank: nc.tensor.matmul(
                            ps_mm[:, bank, :], w1b[:, k, j * 128:(j + 1) * 128],
                            gfT[:, k, :], start=(k == 0), stop=(k == G - 1)),
                            sem="pe" if k == G - 1 else None)
                    B.wait("scalar", "pe", mm)
                    hv = B.sop(lambda e, j=j, bank=bank: nc.scalar.activation(
                        hhb[:, j, :], ps_mm[:, bank, :], ACT.Silu,
                        bias=sb_b1[:, j:j + 1]), sem="sc")
                    hfree[bank] = hv
                B.wait("tensor", "sc", hv)
                for j2 in range(G):
                    bank = j2 % 2
                    B.wait("tensor", "ve", bank_free[bank])
                    for k2 in range(G):
                        mm = B.pop(lambda e, j2=j2, k2=k2, bank=bank: nc.tensor.matmul(
                            ps_mm[:, bank, :], w2b[:, k2, j2 * 128:(j2 + 1) * 128],
                            hhb[:, k2, :], start=(k2 == 0), stop=(k2 == G - 1)),
                            sem="pe" if k2 == G - 1 else None)
                    B.wait("vector", "pe", mm)
                    xv = B.vop(lambda e, j2=j2, bank=bank: nc.vector.tensor_scalar(
                        x2sb[:, j2, :], ps_mm[:, bank, :], sb_b2[:, j2:j2 + 1],
                        None, OP.add, OP.bypass), sem="ve")
                    bank_free[bank] = xv
                for j in range(G):
                    bank = j % 2
                    B.wait("tensor", "ve", bank_free[bank])
                    for k in range(G):
                        mm = B.pop(lambda e, j=j, k=k, bank=bank: nc.tensor.matmul(
                            ps_mm[:, bank, :], w3b[:, k, j * 128:(j + 1) * 128],
                            gfT[:, k, :], start=(k == 0), stop=(k == G - 1)),
                            sem="pe" if k == G - 1 else None)
                    B.wait("vector", "pe", mm)
                    xv3 = B.vop(lambda e, j=j, bank=bank: nc.vector.tensor_scalar(
                        H3sb[:, j, :], ps_mm[:, bank, :], sb_b3[:, j:j + 1],
                        None, OP.add, OP.bypass), sem="ve")
                    bank_free[bank] = xv3
                for j in range(G):
                    yv = B.vop(lambda e, j=j: nc.vector.tensor_tensor(
                        gfT[:, j, :], x2sb[:, j, :], H3sb[:, j, :], OP.mult),
                        sem="ve")
                w_free = B.val("pe")
                if _P3STOP == "fby" and ch == 0:
                    B.wait("gpsimd", "ve")
                    for g in range(G):
                        B.dma("gpsimd", ytok_s[g * 128:(g + 1) * 128, 0:CT],
                              gfT[:, g, :], sem="g")
                    B.wait("gpsimd", "g")
                    B.flush()
                    return "stopdump_y"
                B.wait("tensor", "ve", yv)
                for sub in range(4):
                    yslot = sub % 2
                    for g in range(G):
                        bank = g % 2
                        B.wait("tensor", "ve", state.get(("fby", bank), 0))
                        pv = B.pop(lambda e, g=g, sub=sub, bank=bank:
                                   nc.tensor.transpose(
                            ps_aux[:, bank, 0:128],
                            gfT[:, g, sub * 128:(sub + 1) * 128], sb_eye[:, :]),
                            sem="pe")
                        B.wait("vector", "pe", pv)
                        if g == 0:
                            B.wait("vector", "g", ytk_free[yslot])
                        tv3 = B.vop(lambda e, g=g, yslot=yslot, bank=bank:
                                    nc.vector.tensor_copy(
                            ytk[:, yslot, g * 128:(g + 1) * 128],
                            ps_aux[:, bank, 0:128]), sem="ve")
                        state[("fby", bank)] = tv3
                    B.wait("gpsimd", "ve", tv3)
                    B.wait("gpsimd", "g", ysc_free)
                    inc = B.tick("g", 16)
                    B.emit("gpsimd", lambda e, sub=sub, ch=ch, yslot=yslot:
                           e.indirect_dma_start(
                        ytok_s, IndirectOffsetOnAxis(
                            tcol[:, sub:sub + 1], 0),
                        ytk[:, yslot, :], None, compute_op=OP.add), inc=inc)
                    ysc_free = B.val("g")
                    ytk_free[yslot] = ysc_free
            B.flush()
        if _P3STOP == "ys":
            _emit_dump(B, nc, dram["dbg"], ytok_s[0:D, 0:D], D, cols=D)
            return "stop"
        with B.arena():
            gc = B.sb("fc_gc" + tag, [128, 16, CT])
            sw3b = B.sb("fc_sw3" + tag, [128, 16, D])
            rx = B.sb("fc_rx" + tag, [128, G, CT])
            yy = B.sb("fc_yy" + tag, [128, G, CT])
            ytm = B.sb("fc_ytm" + tag, [128, 4, D])
            wl = B.dma("scalar", sw3b.ap(),
                       sw3.rearrange("(k p) j -> p k j", p=128), sem="w")
            B.wait("tensor", "w", wl)
            buf_free = 0
            ev_free = {0: 0, 1: 0}
            for c in range(NCH):
                B.wait("gpsimd", "ve", buf_free)
                B.wait("vector", "g")
                gl = B.dma("gpsimd", gc.ap(),
                           g_r[:, :, c * CT:(c + 1) * CT], sem="g")
                B.dma("gpsimd", rx.ap(), resid_r[:, :, c * CT:(c + 1) * CT], sem="g")
                gl2 = B.dma("gpsimd", ytm.ap(),
                            ytok_s[c * CT:(c + 1) * CT, :]
                            .rearrange("(sub p) d -> p sub d", p=128), sem="g")
                B.wait("tensor", "g", gl2)
                B.wait("vector", "g", gl2)
                for sub in range(4):
                    for g in range(G):
                        bank = g % 2
                        B.wait("tensor", "ve", state.get(("psbc", bank), 0))
                        pv = B.pop(lambda e, g=g, sub=sub, bank=bank:
                                   nc.tensor.transpose(
                            ps_bc[:, bank, 0:128], ytm[:, sub, g * 128:(g + 1) * 128],
                            sb_eye[:, :]), sem="pe")
                        B.wait("vector", "pe", pv)
                        tvy = B.vop(lambda e, g=g, sub=sub, bank=bank:
                                    nc.vector.tensor_copy(
                            yy[:, g, sub * 128:(sub + 1) * 128],
                            ps_bc[:, bank, 0:128]), sem="ve")
                        state[("psbc", bank)] = tvy
                for j in range(G):
                    bank = j % 2
                    B.wait("tensor", "ve", ev_free[bank])
                    for kk in range(16):
                        mm = B.pop(lambda e, j=j, kk=kk, bank=bank: nc.tensor.matmul(
                            ps_mm[:, bank, :], sw3b[:, kk, j * 128:(j + 1) * 128],
                            gc[:, kk, :], start=(kk == 0), stop=(kk == 15)),
                            sem="pe" if kk == 15 else None)
                    B.wait("vector", "pe", mm)
                    B.vop(lambda e, j=j, bank=bank: nc.vector.tensor_tensor(
                        yy[:, j, :], yy[:, j, :], ps_mm[:, bank, :], OP.add))
                    ev = B.vop(lambda e, j=j: nc.vector.tensor_tensor(
                        yy[:, j, :], yy[:, j, :], rx[:, j, :], OP.add), sem="ve")
                    ev_free[bank] = ev
                    B.wait("gpsimd", "ve", ev)
                    B.dma("gpsimd", out_dst.rearrange("(g p) t -> p g t", p=128)
                          [:, j, c * CT:(c + 1) * CT], yy[:, j, :], sem="g")
                buf_free = B.val("ve")
            B.flush()

    r = emit_ffn(resid_s, "f1")

    if upto == "p3":
        if r == "stopdump":
            _emit_dump(B, nc, dram["dbg"], resid_s, D)
        elif r == "stopdump_y":
            _emit_dump(B, nc, dram["dbg"], ytok_s[0:D, 0:D], D, cols=D)
        elif r != "stop":
            _emit_dump(B, nc, dram["dbg"], resid_s, D)
        B.st.close()
        return nc, outputs

    resid_r = resid_s.rearrange("(g p) t -> p g t", p=128)
    with B.arena():
        rx = B.sb("pa_rx", [128, G, CT])
        xln = B.sb("pa_xln", [128, G, CT])
        wbuf = B.sb("pa_w", [128, 2, G, D // 2])
        cosq = B.sb("pa_cosq", [64, T])
        sinq = B.sb("pa_sinq", [64, T])
        cosk = B.sb("pa_cosk", [64, T])
        sink = B.sb("pa_sink", [64, T])
        t1 = B.sb("pa_t1", [64, CT])
        t2 = B.sb("pa_t2", [64, CT])
        qev = B.sb("pa_qev", [128, 2, CT])
        B.dma("scalar", cosq[:, :], cosq_t, sem="w")
        B.dma("scalar", sinq[:, :], sinq_t, sem="w")
        B.dma("scalar", cosk[:, :], cos_t, sem="w")
        wvt = B.dma("scalar", sink[:, :], sin_t, sem="w")
        B.wait("vector", "w", wvt)
        rx_free = 0
        wh_free = {0: 0, 1: 0}
        ev_free = {0: 0, 1: 0}
        for c in range(NCH):
            B.wait("gpsimd", "ve", rx_free)
            B.wait("vector", "g")
            B.wait("scalar", "g")
            gl = B.dma("gpsimd", rx.ap(), resid_r[:, :, c * CT:(c + 1) * CT], sem="g")
            B.wait("scalar", "g", gl)
            B.wait("tensor", "g", gl)
            B.wait("vector", "g", gl)
            ln_stats(c, rx)
            vv = ln_normalize(xln, rx)
            rx_free = vv
            B.wait("tensor", "ve", vv)
            for which, wsrc, dst, ctab, stab in (
                    (0, w_q, q_s, cosq, sinq), (1, w_k, k_s, cosk, sink)):
                for hf in range(2):
                    buf = hf % 2
                    B.wait("scalar", "pe", wh_free[buf])
                    wl = B.dma("scalar", wbuf[:, buf, :, :],
                               wsrc.rearrange("(g p) j -> p g j", p=128)
                               [:, :, hf * 512:(hf + 1) * 512], sem="w")
                    B.wait("tensor", "w", wl)
                    for hh in range(4):
                        h = hf * 4 + hh
                        bank = hh % 2
                        B.wait("tensor", "ve", ev_free[bank])
                        for k in range(G):
                            mm = B.pop(lambda e, hh=hh, k=k, bank=bank, buf=buf:
                                       nc.tensor.matmul(
                                ps_mm[:, bank, :], wbuf[:, buf, k, hh * 128:(hh + 1) * 128],
                                xln[:, k, :], start=(k == 0), stop=(k == G - 1)),
                                sem="pe" if k == G - 1 else None)
                        wh_free[buf] = mm
                        B.wait("vector", "pe", mm)
                        csl = ctab[:, c * CT:(c + 1) * CT]
                        ssl = stab[:, c * CT:(c + 1) * CT]
                        B.vop(lambda e, bank=bank, csl=csl: nc.vector.tensor_tensor(
                            t1[:, :], ps_mm[0:64, bank, :], csl, OP.mult))
                        B.vop(lambda e, bank=bank, ssl=ssl: nc.vector.tensor_tensor(
                            t2[:, :], ps_mm[64:128, bank, :], ssl, OP.mult))
                        B.vop(lambda e, bank=bank: nc.vector.tensor_tensor(
                            qev[0:64, bank, :], t1[:, :], t2[:, :], OP.subtract))
                        B.vop(lambda e, bank=bank, ssl=ssl: nc.vector.tensor_tensor(
                            t1[:, :], ps_mm[0:64, bank, :], ssl, OP.mult))
                        B.vop(lambda e, bank=bank, csl=csl: nc.vector.tensor_tensor(
                            t2[:, :], ps_mm[64:128, bank, :], csl, OP.mult))
                        ev = B.vop(lambda e, bank=bank: nc.vector.tensor_tensor(
                            qev[64:128, bank, :], t1[:, :], t2[:, :], OP.add),
                            sem="ve")
                        ev_free[bank] = ev
                        B.wait("gpsimd", "ve", ev)
                        B.dma("gpsimd", dst[h * 128:(h + 1) * 128,
                                            c * CT:(c + 1) * CT],
                              qev[:, bank, :], sem="g")
            for hf in range(2):
                buf = hf % 2
                B.wait("scalar", "pe", wh_free[buf])
                wl = B.dma("scalar", wbuf[:, buf, :, :],
                           w_v.rearrange("(g p) j -> p g j", p=128)
                           [:, :, hf * 512:(hf + 1) * 512], sem="w")
                B.wait("tensor", "w", wl)
                for hh in range(4):
                    h = hf * 4 + hh
                    for sub in range(4):
                        bank = sub % 2
                        B.wait("tensor", "ve", ev_free[bank])
                        for k in range(G):
                            mm = B.pop(lambda e, hh=hh, k=k, sub=sub, bank=bank, buf=buf:
                                       nc.tensor.matmul(
                                ps_mm[:, bank, 0:128],
                                xln[:, k, sub * 128:(sub + 1) * 128],
                                wbuf[:, buf, k, hh * 128:(hh + 1) * 128],
                                start=(k == 0), stop=(k == G - 1)),
                                sem="pe" if k == G - 1 else None)
                        wh_free[buf] = mm
                        B.wait("vector", "pe", mm)
                        ev = B.vop(lambda e, bank=bank: nc.vector.tensor_copy(
                            qev[:, bank, 0:128], ps_mm[:, bank, 0:128]), sem="ve")
                        ev_free[bank] = ev
                        B.wait("gpsimd", "ve", ev)
                        B.dma("gpsimd",
                              v_s[h, c * CT + sub * 128: c * CT + (sub + 1) * 128, :],
                              qev[:, bank, 0:128], sem="g")
        B.flush()
    if upto == "p4" and _P4STOP == "qkv":
        _emit_dump(B, nc, dram["dbg"], q_s, D)
        B.st.close()
        return nc, outputs

    with B.arena():
        kT = B.sb("pb_kT", [128, T])
        qT = B.sb("pb_qT", [128, T])
        Vt = B.sb("pb_V", [128, 32, DH])
        Ssb = B.sb("pb_S", [128, T])
        PTall = B.sb("pb_PT", [128, 32, 128])
        osb = B.sb("pb_o", [128, 128])
        mcol = B.sb("pb_mcol", [128, 1])
        rrow = B.sb("pb_rrow", [1, 128])
        recb = B.sb("pb_recb", [128, 128])

        for h in range(HEADS):
            B.wait("gpsimd", "pe")
            B.wait("gpsimd", "ve")
            gl0 = B.dma("gpsimd", kT[:, :], k_s[h * 128:(h + 1) * 128, :], sem="g")
            B.dma("gpsimd", qT[:, :], q_s[h * 128:(h + 1) * 128, :], sem="g")
            gl = B.dma("gpsimd", Vt.ap(),
                       v_s[h].rearrange("(s p) d -> p s d", p=128), sem="g")
            B.wait("tensor", "g", gl)
            B.wait("vector", "g", gl)
            for qi in range(T // 128):
                width = (qi + 1) * 128
                sv_free = {0: 0, 1: 0}
                for jb0 in range(0, width, 512):
                    nb = min(512, width - jb0)
                    bank = (jb0 // 512) % 2
                    B.wait("tensor", "ve", sv_free[bank])
                    mm = B.pop(lambda e, qi=qi, jb0=jb0, nb=nb, bank=bank:
                               nc.tensor.matmul(
                        ps_mm[:, bank, 0:nb], qT[:, qi * 128:(qi + 1) * 128],
                        kT[:, jb0:jb0 + nb], start=True, stop=True), sem="pe")
                    B.wait("vector", "pe", mm)
                    ev = B.vop(lambda e, jb0=jb0, nb=nb, bank=bank:
                               nc.vector.tensor_copy(
                        Ssb[:, jb0:jb0 + nb], ps_mm[:, bank, 0:nb]), sem="ve")
                    sv_free[bank] = ev
                B.vop(lambda e, qi=qi: nc.vector.tensor_tensor(
                    Ssb[:, qi * 128:(qi + 1) * 128],
                    Ssb[:, qi * 128:(qi + 1) * 128], sb_caus[:, :], OP.add))
                vv = B.vop(lambda e, width=width: nc.vector.tensor_reduce(
                    mcol[:, 0:1], Ssb[:, 0:width], AX, OP.max, negate=True),
                    sem="ve")
                B.wait("scalar", "ve", vv)
                sv = B.sop(lambda e, width=width: nc.scalar.activation(
                    Ssb[:, 0:width], Ssb[:, 0:width], ACT.Exp, bias=mcol[:, 0:1]),
                    sem="sc")
                B.wait("tensor", "sc", sv)
                B.wait("vector", "sc", sv)
                tr_free = {0: 0, 1: 0}
                for jt in range(qi + 1):
                    bank = jt % 2
                    B.wait("tensor", "ve", tr_free[bank])
                    pv = B.pop(lambda e, jt=jt, bank=bank: nc.tensor.transpose(
                        ps_aux[:, bank, 0:128], Ssb[:, jt * 128:(jt + 1) * 128],
                        sb_eye[:, :]), sem="pe")
                    B.wait("vector", "pe", pv)
                    tv = B.vop(lambda e, jt=jt, bank=bank: nc.vector.tensor_copy(
                        PTall[:, jt, :], ps_aux[:, bank, 0:128]), sem="ve")
                    tr_free[bank] = tv
                B.wait("tensor", "ve", tv)
                for jt in range(qi + 1):
                    mm = B.pop(lambda e, jt=jt, qi=qi: nc.tensor.matmul(
                        ps_bc[0:1, 1, 0:128], sb_ones_col[:, :], PTall[:, jt, :],
                        start=(jt == 0), stop=(jt == qi)),
                        sem="pe" if jt == qi else None)
                B.wait("vector", "pe", mm)
                rv = B.vop(lambda e: nc.vector.reciprocal(
                    rrow[0:1, :], ps_bc[0:1, 1, 0:128]), sem="ve")
                B.wait("tensor", "ve", rv)
                bv = B.pop(lambda e: nc.tensor.matmul(
                    ps_bc[:, 1, 0:128], sb_ones_row[0:1, :], rrow[0:1, :],
                    start=True, stop=True), sem="pe")
                B.wait("vector", "pe", bv)
                rbv = B.vop(lambda e: nc.vector.tensor_copy(
                    recb[:, :], ps_bc[:, 1, 0:128]), sem="ve")
                for jt in range(qi + 1):
                    mm = B.pop(lambda e, jt=jt, qi=qi: nc.tensor.matmul(
                        ps_bc[:, 0, 0:128], Vt[:, jt, :], PTall[:, jt, :],
                        start=(jt == 0), stop=(jt == qi)),
                        sem="pe" if jt == qi else None)
                B.wait("vector", "pe", mm)
                ov = B.vop(lambda e: nc.vector.tensor_tensor(
                    osb[:, :], ps_bc[:, 0, 0:128], recb[:, :], OP.mult),
                    sem="ve")
                B.wait("gpsimd", "ve", ov)
                B.dma("gpsimd", o_s[h * 128:(h + 1) * 128, qi * 128:(qi + 1) * 128],
                      osb[:, :], sem="g")
        B.flush()

    with B.arena():
        oc = B.sb("pc_oc", [128, G, CT])
        rx = B.sb("pc_rx", [128, G, CT])
        wob = B.sb("pc_wo", [128, G, D])
        wl = B.dma("scalar", wob.ap(), w_o.rearrange("(g p) j -> p g j", p=128),
                   sem="w")
        B.wait("tensor", "w", wl)
        buf_free = 0
        ev_free = {0: 0, 1: 0}
        for c in range(NCH):
            B.wait("gpsimd", "ve", buf_free)
            B.wait("vector", "g")
            gl = B.dma("gpsimd", oc.ap(),
                       o_s.rearrange("(g p) t -> p g t", p=128)
                       [:, :, c * CT:(c + 1) * CT], sem="g")
            gl2 = B.dma("gpsimd", rx.ap(), resid_r[:, :, c * CT:(c + 1) * CT], sem="g")
            B.wait("tensor", "g", gl)
            B.wait("vector", "g", gl2)
            for j in range(G):
                bank = j % 2
                B.wait("tensor", "ve", ev_free[bank])
                for k in range(G):
                    mm = B.pop(lambda e, j=j, k=k, bank=bank: nc.tensor.matmul(
                        ps_mm[:, bank, :], wob[:, k, j * 128:(j + 1) * 128],
                        oc[:, k, :], start=(k == 0), stop=(k == G - 1)),
                        sem="pe" if k == G - 1 else None)
                B.wait("vector", "pe", mm)
                ev = B.vop(lambda e, j=j, bank=bank: nc.vector.tensor_tensor(
                    rx[:, j, :], rx[:, j, :], ps_mm[:, bank, :], OP.add), sem="ve")
                ev_free[bank] = ev
                B.wait("gpsimd", "ve", ev)
                B.dma("gpsimd", resid_r[:, j, c * CT:(c + 1) * CT], rx[:, j, :],
                      sem="g")
            buf_free = B.val("ve")
        B.flush()

    if upto == "p4":
        _emit_dump(B, nc, dram["dbg"], resid_s, D)
        B.st.close()
        return nc, outputs

    emit_ffn(dram["outT"], "f2")
    B.wait("gpsimd", "g")
    B.flush()
    B.st.close()
    return nc, outputs


def _emit_dump(B, nc, dbg_ap, src_ap, rows, cols=T):
    tile = B.sb("dump_tile", [128, CT])
    for r in range(rows // 128):
        for c in range(cols // CT):
            lv = B.dma("gpsimd", tile[:, :],
                       src_ap[r * 128:(r + 1) * 128, c * CT:(c + 1) * CT], sem="g")
            B.wait("gpsimd", "g", lv)
            B.dma("gpsimd", dbg_ap[r * 128:(r + 1) * 128, c * CT:(c + 1) * CT],
                  tile[:, :], sem="g")
    B.wait("gpsimd", "g")
    B.flush()



def _prep_shared(i):
    t = np.ascontiguousarray
    E8 = E
    w = {}
    w["w_in"] = t(i["L0_attn_a_in_proj_w"].T)
    w["b_in"] = t(i["L0_attn_a_in_proj_b"].reshape(16, 128).T)
    w["conv_w"] = t(i["L0_attn_a_conv_w"].reshape(D, 4).reshape(G, 128, 4)
                    .transpose(1, 0, 2).reshape(128, G * 4))
    w["conv_b"] = t(i["L0_attn_a_conv_b"].reshape(G, 128).T)
    w["A_log"] = t(i["L0_attn_a_A_log"].reshape(G, 128, NST).transpose(1, 0, 2)
                   .reshape(128, G * NST))
    w["w_B"] = t(i["L0_attn_a_B_w"].T)
    w["b_B"] = t(i["L0_attn_a_B_b"].reshape(NST, 1))
    w["w_C"] = t(i["L0_attn_a_C_w"].T)
    w["b_C"] = t(i["L0_attn_a_C_b"].reshape(NST, 1))
    w["w_dt"] = t(i["L0_attn_a_dt_w"].T)
    w["b_dt"] = t(i["L0_attn_a_dt_b"].reshape(G, 128).T)
    w["w_out"] = t(i["L0_attn_a_out_w"].T)
    w["b_out"] = t(i["L0_attn_a_out_b"].reshape(G, 128).T)
    w["ln_g"] = t(i["L0_attn_a_ln_g"].reshape(G, 128).T)
    w["ln_b"] = t(i["L0_attn_a_ln_beta"].reshape(G, 128).T)
    w["w_q"] = t(i["L0_attn_b_w_aq"].reshape(D, D))
    w["w_k"] = t(i["L0_attn_b_w_ak"].reshape(D, D))
    w["w_v"] = t(i["L0_attn_b_w_av"].reshape(D, D))
    w["w_o"] = t(i["L0_attn_b_w_ao"].reshape(D, D))
    w["gate_w"] = t(i["L0_ffn_gate_w"].T)
    w["w1"] = t(np.swapaxes(i["L0_ffn_w1"], 1, 2))
    w["b1"] = t(i["L0_ffn_b1"].reshape(E8, G, 128).transpose(0, 2, 1))
    w["w2"] = t(np.swapaxes(i["L0_ffn_w2"], 1, 2))
    w["b2"] = t(i["L0_ffn_b2"].reshape(E8, G, 128).transpose(0, 2, 1))
    w["w3"] = t(np.swapaxes(i["L0_ffn_w3"], 1, 2))
    w["b3"] = t(i["L0_ffn_b3"].reshape(E8, G, 128).transpose(0, 2, 1))
    w["sw1"] = t(i["L0_ffn_sw1"].T)
    w["sw2"] = t(i["L0_ffn_sw2"].T)
    w["sw3"] = t(i["L0_ffn_sw3"].T)
    w["eye"] = np.eye(128, dtype=np.float32)
    ii = np.arange(128)
    w["caus"] = np.where(ii[:, None] < ii[None, :], np.float32(-1e30),
                         np.float32(0.0)).astype(np.float32)
    pos = np.arange(T, dtype=np.float64)
    dims = np.arange(64, dtype=np.float64)
    ang = np.power(10000.0, -dims / 64.0)
    rad = pos[None, :] * ang[:, None]
    w["cos_t"] = np.cos(rad).astype(np.float32)
    w["sin_t"] = np.sin(rad).astype(np.float32)
    w["cosq_t"] = (np.cos(rad) / DH).astype(np.float32)
    w["sinq_t"] = (np.sin(rad) / DH).astype(np.float32)
    tokid = (np.arange(32)[None, :] * 128 + np.arange(128)[:, None]).astype(np.int32)
    eb = np.zeros((40, 1), dtype=np.float32)
    for e in range(8):
        eb[e, 0] = e * 1536 - 1
        eb[32 + e, 0] = e * 1536 - 1
    for k in w:
        w[k] = np.ascontiguousarray(w[k], dtype=np.float32)
    w["tokid"] = tokid
    w["ebase"] = eb
    return w


def kernel(**inputs):
    shared = _prep_shared(inputs)
    x = inputs["L0_attn_a_x"]
    nc, outputs = build_nc(_UPTO)
    in_maps = []
    for b in range(_NCORES):
        m = dict(shared)
        m["xT"] = np.ascontiguousarray(x[b].T)
        in_maps.append(m)
    res = run_bass_kernel_spmd(nc, in_maps, core_ids=list(range(_NCORES)))
    if _UPTO != "full":
        return [r["dbg"] for r in res.results]
    out = np.empty((_NCORES, T, D), dtype=np.float32)
    for b in range(_NCORES):
        out[b] = res.results[b]["outT"].T
    return out
